# version 01 (final)
# baseline (speedup 1.0000x reference)
import os
import numpy as np
from contextlib import ExitStack
import ml_dtypes
import concourse.bass as bass
import concourse.mybir as mybir
from concourse.bass_utils import run_bass_kernel_spmd

F32 = mybir.dt.float32
BF16 = mybir.dt.bfloat16
ALU = mybir.AluOpType
AF = mybir.ActivationFunctionType
AX = mybir.AxisListType

C, H, D, HPC, NE = 1024, 16, 64, 4, 8
NMICH = 12


class S:
    def __init__(self, h):
        self.h, self.n = h, 0

    def inc(self, instr, k=1):
        instr.then_inc(self.h, k)
        self.n += k
        return self.n


def build_A(T, races=True, stage=5):
    nc = bass.Bass(target_bir_lowering=False, detect_race_conditions=races)
    xfull = nc.dram_tensor("xfull", [T, C], F32, kind="ExternalInput").ap()
    wqkv = nc.dram_tensor("wqkv", [C, 768], F32, kind="ExternalInput").ap()
    bqkv = nc.dram_tensor("bqkv", [1, 768], F32, kind="ExternalInput").ap()
    y_out = nc.dram_tensor("y_out", [HPC, T, D], BF16, kind="ExternalOutput").ap()
    xbf = nc.dram_tensor("xbf", [T, C], BF16).ap()
    qtm = nc.dram_tensor("qtm", [HPC, T, 128], BF16).ap()
    vtm = nc.dram_tensor("vtm", [HPC, 128, T // 128 * D], BF16).ap()
    khat = nc.dram_tensor("khat", [HPC, 128, T], BF16).ap()
    gqd = nc.dram_tensor("gqd", [128, 4], F32).ap()
    gq2 = nc.dram_tensor("gq2", [1, 4], F32).ap()

    TC = 1024
    NSUB = TC // 128
    NCH = T // TC
    NT = T // 128
    NQT = T // 512
    NKC = T // 128

    def chunks(g):
        return min(4 * (g + 1), NKC)

    def isdiag(g, c):
        return c >= 4 * g

    units = [(h, g) for h in range(HPC) for g in range(NQT)]

    MM_QKV = NCH * NSUB
    MM_EXPB = MM_QKV + 1
    MMP = {}
    mm_n = MM_EXPB
    SQ_EXPB = 1
    sq_n = 1
    SQ_VHEXP, SQ_EXP = {}, {}
    AVCC = {}
    avc_n = 0
    cg_of = {}
    cg = 0
    for h in range(HPC):
        if stage >= 5:
            sq_n += 2
            SQ_VHEXP[h] = sq_n
        for g in range(NQT):
            nch_ = chunks(g)
            for c in range(nch_):
                cg += 1
                cg_of[(h, g, c)] = cg
                if c % 2 == 1:
                    mm_n += 1
                    MMP[(h, g, c // 2)] = mm_n
                sq_n += 1
                SQ_EXP[(h, g, c)] = sq_n
                avc_n += 1
                AVCC[(h, g, c)] = avc_n
    AVC_BY_CG = {cg_of[k]: AVCC[k] for k in cg_of}
    SQ_BY_CG = {cg_of[k]: SQ_EXP[k] for k in cg_of}

    ST_STAGE = {tc: 16 * 12 * (tc + 1) for tc in range(NCH)}
    st_base = 16 * 12 * NCH
    ST_KHAT = {h: st_base + 16 * (h + 1) for h in range(HPC)}
    st_base2 = st_base + 16 * HPC
    ST_Y = {}
    for i, (hh, gg) in enumerate(units):
        ST_Y[(hh, gg)] = st_base2 + 16 * (i + 1)
    LD_IN = 16 * 10
    LD_M14 = 16 * 14
    LD_VAUG = {h: 16 * (14 + (h + 1)) for h in range(HPC)}
    TR_XFM = {tc: 16 * 8 * (tc + 1) for tc in range(NCH)}
    TR_KQ = {h: 16 * (8 * NCH + 2 * (h + 1)) for h in range(HPC)}

    VQP = {}

    with (
        nc.sbuf_tensor("arena", [128, 45500], F32) as AR,
        nc.sbuf_tensor("wq_sb", [128, 8 * 768], BF16) as wq_sb,
        nc.sbuf_tensor("bias_sb", [1, 768], BF16) as bias_sb,
        nc.sbuf_tensor("ones1", [1, 128], BF16) as ones1,
        nc.sbuf_tensor("m14", [1, 4], BF16) as m14,
        nc.sbuf_tensor("masks", [128, 4 * 512], BF16) as masks_sb,
        nc.sbuf_tensor("onesb", [128, 512], BF16) as onesb,
        nc.sbuf_tensor("gq", [128, 4], F32) as gq,
        nc.sbuf_tensor("gq_t", [4, 128], F32) as gq_t,
        nc.sbuf_tensor("gqm", [4, 1], F32) as gqm,
        nc.sbuf_tensor("expb", [128, 4], F32) as expb,
        nc.sbuf_tensor("sec", [128, 4 * 64], F32) as sec,
        nc.psum_tensor("PS", [128, 4096], F32) as PS,
        ExitStack() as _es,
        nc.Block() as block,
    ):
        _semnames = ["ld", "tr", "mm", "sq", "vq", "gd", "st", "avc"]
        _sems = [_es.enter_context(nc.semaphore(n + "_")) for n in _semnames]
        ld, tr, mm, sq, vq, gd, st, avc = (S(x) for x in _sems)
        ps_a, ps_b = PS[:, 0:512], PS[:, 512:768]
        ps_a2, ps_b2 = PS[:, 1024:1536], PS[:, 1536:1792]
        ps_st, ps_st2 = PS[:, 0:1024], PS[:, 1024:2048]
        ps_bc = PS[:, 2048:2052]

        def ps_av(sub):
            return PS[:, 2048 + sub * 512:2048 + sub * 512 + 65]

        o, off = {}, 0

        def alloc(n_, sz):
            nonlocal off
            o[n_] = off
            off += sz
        alloc("xfm", 4096)
        alloc("qstage", 1024)
        alloc("vstage", 1024)
        o["vstg2"] = o["qstage"]
        alloc("kbuf", NT * 256)
        o["scr"] = o["xfm"]
        alloc("khtmp", NT * 64)
        alloc("kqfm", T)
        alloc("vaug", (NKC * 65 + 1) // 2)
        alloc("vhat", NKC * 32)
        alloc("pt", 1024)
        alloc("ystage", 512)
        alloc("sc", 512)
        assert off <= 45500, f"arena overflow {off}"

        def f32v(name, a, n):
            return AR[:, o[name] + a:o[name] + a + n]

        def bf16v(name, a, n):
            return AR.bitcast(BF16)[:, 2 * o[name] + a:2 * o[name] + a + n]

        def xfm_d(d):
            return bf16v("xfm", d * TC, TC)

        def qstage(par):
            return bf16v("qstage", 0, 2048)

        def vstage(par):
            return bf16v("vstage", 0, 2048)

        def kfm(slot):
            return bf16v("kqfm", 0, T)

        def qfm(slot):
            return bf16v("kqfm", T, T)

        def vaug(slot):
            return bf16v("vaug", 0, NKC * 65)

        def vhat(slot):
            return bf16v("vhat", 0, NKC * 64)

        def ptb(i):
            return bf16v("pt", i * 512, 512)

        def ystage(i):
            return bf16v("ystage", (i % 4) * 256, 256)
        kb3 = f32v("kbuf", 0, NT * 256).rearrange("p (c w) -> p c w", w=256)
        scr3 = f32v("scr", 0, NT * 64).rearrange("p (c w) -> p c w", w=64)
        scrf = f32v("scr", 0, NT * 64)
        tb = sec[:, 0:NT]
        fn = sec[:, 64:64 + NT]
        cn = sec[:, 128:128 + NT]
        u_ = sec[:, 192:192 + NT]
        n2col = f32v("sc", 64, NT)
        sc4 = f32v("sc", 0, 4)
        yn = bf16v("sc", 2 * 128, 64)
        prod = f32v("sc", 192, 64)
        cf = f32v("sc", 256, 1)
        rcp = f32v("sc", 257, 1)

        def tbc(ap2):
            return ap2.rearrange("p (c x) -> p c x", x=1).broadcast_to((128, NT, 64))

        @block.vector
        def _(v):
            def F(instr):
                return vq.inc(instr)

            def W():
                v.wait_ge(vq.h, vq.n)
            for ti in range(NT):
                tc = ti // NSUB
                par = tc % 2
                v.wait_ge(mm.h, ti + 1)
                if tc >= 1:
                    v.wait_ge(st.h, ST_STAGE[tc - 1])
                pa, pb = (ps_a, ps_b) if ti % 2 == 0 else (ps_a2, ps_b2)
                F(v.tensor_copy(f32v("kbuf", ti * 256, 256), pa[:, 256:512]))
                F(v.tensor_copy(qstage(par)[:, (ti % NSUB) * 256:(ti % NSUB + 1) * 256], pa[:, 0:256]))
                F(v.tensor_copy(vstage(par)[:, (ti % NSUB) * 256:(ti % NSUB + 1) * 256], pb[:, 0:256]))
                W()
                F(v.tensor_reduce(sc4, pa[:, 0:256].rearrange("p (h d) -> p h d", d=64),
                                  AX.X, ALU.max, apply_absolute_value=True))
                W()
                if ti == 0:
                    F(v.tensor_copy(gq[:, :], sc4))
                else:
                    F(v.tensor_tensor(gq[:, :], gq[:, :], sc4, ALU.max))
                VQP[("vd", ti)] = vq.n
            if stage >= 2:
                v.wait_ge(ld.h, 16 * 12)
                F(v.tensor_reduce(gqm[:, :], gq_t[:, :], AX.X, ALU.max))
                VQP[("gx",)] = vq.n
            if stage >= 3:
                for h in range(HPC):
                    zv = kb3[:, :, h * 64:(h + 1) * 64]
                    W()
                    F(v.tensor_reduce(fn, zv, AX.X, ALU.add))
                    W()
                    F(v.tensor_scalar(tb, fn, 1.0 / 64, -1.0 / 64, ALU.mult, ALU.add))
                    for it in range(NMICH):
                        W()
                        F(v.tensor_tensor(scr3, zv, tbc(tb), ALU.subtract))
                        W()
                        F(v.tensor_scalar(scrf, scrf, 0.0, None, ALU.max))
                        W()
                        F(v.tensor_reduce(fn, scr3, AX.X, ALU.add))
                        W()
                        F(v.tensor_scalar(fn, fn, -1.0, None, ALU.add))
                        W()
                        F(v.tensor_scalar(scrf, scrf, 0.0, None, ALU.is_gt))
                        W()
                        F(v.tensor_reduce(cn, scr3, AX.X, ALU.add))
                        W()
                        F(v.reciprocal(u_, cn))
                        W()
                        F(v.tensor_tensor(u_, fn, u_, ALU.mult))
                        W()
                        F(v.tensor_tensor(tb, tb, u_, ALU.add))
                    W()
                    F(v.tensor_tensor(scr3, zv, tbc(tb), ALU.subtract))
                    if h >= 1:
                        v.wait_ge(st.h, ST_KHAT[h - 1])
                    kh3 = bf16v("khtmp", 0, NT * 128).rearrange("p (c x) -> p c x", x=128)
                    W()
                    F(v.tensor_scalar(kh3[:, :, 0:64], scr3, 0.0, None, ALU.max))
                    W()
                    F(v.tensor_scalar(kh3[:, :, 64:128], scr3, 0.0, None, ALU.max))
                    VQP[("sp", h)] = vq.n
            if stage >= 5:
                v.wait_ge(gd.h, 6)
                for h in range(HPC):
                    slot = h % 2
                    v.wait_ge(ld.h, LD_VAUG[h])
                    vdat = vaug(slot).rearrange("p (c w) -> p c w", w=65)[:, :, 0:64]
                    W()
                    F(v.tensor_copy(vdat, bf16v("vstg2", 0, NT * 64).rearrange("p (c w) -> p c w", w=64)))
                    VQP[("vld", h)] = vq.n
                    W()
                    F(v.tensor_tensor(scr3, vdat, vdat, ALU.mult))
                    W()
                    F(v.tensor_reduce(n2col, scr3, AX.X, ALU.add))
                    W()
                    F(v.tensor_scalar(n2col, n2col, 1e-24, None, ALU.max))
                    VQP[("vhg", h)] = vq.n
                    v.wait_ge(sq.h, SQ_VHEXP[h])
                    W()
                    F(v.tensor_tensor(vhat(slot).rearrange("p (c w) -> p c w", w=64),
                                      vdat, tbc(n2col), ALU.mult))
                    VQP[("vhf", h)] = vq.n
                    for g_ in range(NQT):
                        nch_ = chunks(g_)
                        for c in range(nch_):
                            if isdiag(g_, c):
                                cgx = cg_of[(h, g_, c)]
                                v.wait_ge(sq.h, SQ_EXP[(h, g_, c)])
                                W()
                                F(v.tensor_tensor(ptb(cgx % 4), ptb(cgx % 4),
                                                  masks_sb[:, (c - 4 * g_) * 512:(c - 4 * g_ + 1) * 512],
                                                  ALU.mult))
                                VQP[("msk", h, g_, c)] = vq.n
                        v.wait_ge(avc.h, AVCC[(h, g_, chunks(g_) - 1)])
                        k_ = h * NQT + g_
                        if k_ >= 4:
                            v.wait_ge(st.h, ST_Y[units[(k_ - 4) | 1]])
                        for sub in range(4):
                            tt_ = g_ * 4 + sub
                            vh_sl = vhat(slot)[:, tt_ * 64:(tt_ + 1) * 64]
                            W()
                            F(v.reciprocal(rcp, ps_av(sub)[:, 64:65]))
                            W()
                            F(v.tensor_scalar(yn, ps_av(sub)[:, 0:64], rcp, None, ALU.mult))
                            W()
                            F(v.tensor_tensor(prod, yn, vh_sl, ALU.mult))
                            W()
                            F(v.tensor_reduce(cf, prod, AX.X, ALU.add, negate=True))
                            W()
                            F(v.scalar_tensor_tensor(ystage(k_)[:, sub * 64:(sub + 1) * 64],
                                                     vh_sl, cf, yn, ALU.mult, ALU.add))
                        VQP[("yd", h, g_)] = vq.n

        def VD(ti):
            return VQP[("vd", ti)]

        @block.gpsimd
        def _(g):
            for d in range(8):
                ld.inc(g.dma_start(out=wq_sb[:, d * 768:(d + 1) * 768],
                                   in_=wqkv[d * 128:(d + 1) * 128, :]), 16)
            ld.inc(g.dma_start(out=bias_sb[:, :], in_=bqkv), 16)
            ld.inc(g.dma_start(out=xbf[:, :], in_=xfull), 16)
            assert ld.n == LD_IN
            gd.inc(g.memset(onesb[:, :], 1.0))
            gd.inc(g.memset(ones1[:, :], 1.0))
            g.wait_ge(gd.h, gd.n)
            for j in range(4):
                gd.inc(g.affine_select(masks_sb[:, j * 512:(j + 1) * 512], onesb[:, :],
                                       pattern=[[1, 512]], compare_op=ALU.is_ge,
                                       fill=0.0, base=-128 * j, channel_multiplier=-1))
            if stage < 2:
                return
            g.wait_ge(vq.h, VD(NT - 1))
            ld.inc(g.dma_start(out=gqd, in_=gq[:, :]), 16)
            g.wait_ge(ld.h, ld.n)
            ld.inc(g.dma_start(out=gq_t[:, :], in_=gqd.rearrange("p h -> h p"),
                               allow_slow_non_contiguous=True), 16)
            g.wait_ge(vq.h, VQP[("gx",)])
            ld.inc(g.dma_start(out=gq2.rearrange("x h -> h x"), in_=gqm[:, :],
                               allow_slow_non_contiguous=True), 16)
            g.wait_ge(ld.h, ld.n)
            ld.inc(g.dma_start(out=m14[:, :], in_=gq2), 16)
            assert ld.n == LD_M14
            g.wait_ge(ld.h, ld.n)
            if stage < 4:
                return
            for h in range(HPC):
                g.wait_ge(st.h, ST_STAGE[NCH - 1])
                if h >= 1:
                    g.wait_ge(avc.h, AVCC[(h - 1, NQT - 1, chunks(NQT - 1) - 1)])
                gd.inc(g.memset(vaug(0), 1.0))
                g.wait_ge(gd.h, gd.n)
                ld.inc(g.dma_start(out=bf16v("vstg2", 0, NT * 64), in_=vtm[h]), 16)
                assert ld.n == LD_VAUG[h]
                g.wait_ge(ld.h, ld.n)

        @block.sync
        def _(sy):
            sy.wait_ge(ld.h, LD_IN)
            for tc in range(NCH):
                if tc >= 1:
                    sy.wait_ge(mm.h, tc * NSUB)
                for d in range(8):
                    tr.inc(sy.dma_start(out=xfm_d(d),
                                        in_=xbf[tc * TC:(tc + 1) * TC, d * 128:(d + 1) * 128],
                                        transpose=True), 16)
                assert tr.n == TR_XFM[tc]
                sy.wait_ge(tr.h, tr.n)
                sy.wait_ge(vq.h, VD((tc + 1) * NSUB - 1))
                par = tc % 2
                for h in range(HPC):
                    for du in range(2):
                        st.inc(sy.dma_start(
                            out=qtm[h][tc * TC:(tc + 1) * TC, du * 64:(du + 1) * 64].rearrange("(s p) d -> p s d", p=128),
                            in_=qstage(par).rearrange("p (s w) -> p s w", w=256)[:, :, h * 64:(h + 1) * 64]), 16)
                    st.inc(sy.dma_start(
                        out=vtm[h][:, tc * NSUB * 64:(tc + 1) * NSUB * 64].rearrange("p (s d) -> p s d", d=64),
                        in_=vstage(par).rearrange("p (s w) -> p s w", w=256)[:, :, h * 64:(h + 1) * 64]), 16)
                assert st.n == ST_STAGE[tc]
                sy.wait_ge(st.h, st.n)
            if stage < 3:
                sy.wait_ge(st.h, st.n)
                return
            for h in range(HPC):
                sy.wait_ge(vq.h, VQP[("sp", h)])
                st.inc(sy.dma_start(out=khat[h], in_=bf16v("khtmp", 0, NT * 128)), 16)
                assert st.n == ST_KHAT[h]
                sy.wait_ge(st.h, st.n)
            def emit_y(h):
                for g_ in range(NQT):
                    k_ = h * NQT + g_
                    sy.wait_ge(vq.h, VQP[("yd", h, g_)])
                    st.inc(sy.dma_start(
                        out=y_out[h][g_ * 512:(g_ + 1) * 512].rearrange("(s p) d -> p s d", p=128),
                        in_=ystage(k_).rearrange("p (s d) -> p s d", d=64)), 16)
                    assert st.n == ST_Y[(h, g_)]
                    if k_ % 2 == 1:
                        sy.wait_ge(st.h, st.n)
            if stage >= 4:
                for h in range(HPC):
                    if stage >= 5 and h >= 1:
                        emit_y(h - 1)
                    if h >= 1:
                        sy.wait_ge(vq.h, VQP.get(("yd", h - 1, NQT - 1), 0))
                    tr.inc(sy.dma_start(out=kfm(0)[:, :].rearrange("x (c p) -> x c p", p=128),
                                        in_=khat[h].rearrange("p (c x) -> p c x", x=128),
                                        transpose=True), 16)
                    tr.inc(sy.dma_start(out=qfm(0)[:, :], in_=qtm[h], transpose=True), 16)
                    assert tr.n == TR_KQ[h]
                    sy.wait_ge(tr.h, tr.n)
            if stage >= 5:
                emit_y(HPC - 1)
            sy.wait_ge(st.h, st.n)

        @block.tensor
        def _(t):
            t.wait_ge(ld.h, LD_IN)
            t.wait_ge(gd.h, 2)
            for tc in range(NCH):
                t.wait_ge(tr.h, TR_XFM[tc])
                for s_ in range(NSUB):
                    ti = tc * NSUB + s_
                    pa, pb = (ps_a, ps_b) if ti % 2 == 0 else (ps_a2, ps_b2)
                    if ti >= 2:
                        t.wait_ge(vq.h, VD(ti - 1))
                    for d in range(8):
                        t.matmul(pa[:, 0:512], xfm_d(d)[:, s_ * 128:(s_ + 1) * 128],
                                 wq_sb[:, d * 768:d * 768 + 512], start=(d == 0), stop=False)
                        t.matmul(pb[:, 0:256], xfm_d(d)[:, s_ * 128:(s_ + 1) * 128],
                                 wq_sb[:, d * 768 + 512:d * 768 + 768], start=(d == 0), stop=False)
                    t.matmul(pa[:, 0:512], ones1[0:1, :], bias_sb[0:1, 0:512], start=False, stop=True)
                    mm.inc(t.matmul(pb[:, 0:256], ones1[0:1, :], bias_sb[0:1, 512:768],
                                    start=False, stop=True))
            assert mm.n == MM_QKV
            if stage < 2:
                return
            t.wait_ge(vq.h, VD(NT - 1))
            t.wait_ge(ld.h, LD_M14)
            mm.inc(t.matmul(ps_bc[:, :], ones1[0:1, :], m14[0:1, :], start=True, stop=True))
            assert mm.n == MM_EXPB
            if stage < 5:
                return
            for h in range(HPC):
                slot = h % 2
                t.wait_ge(tr.h, TR_KQ[h])
                t.wait_ge(vq.h, VQP[("vld", h)])
                for g_ in range(NQT):
                    nch_ = chunks(g_)
                    if (h, g_) != (0, 0):
                        prev = units[units.index((h, g_)) - 1]
                        t.wait_ge(vq.h, VQP[("yd", prev[0], prev[1])])
                    for pr0 in range(0, nch_, 2):
                        for c in (pr0, pr0 + 1):
                            cgx = cg_of[(h, g_, c)]
                            half = (cgx - 1) % 2
                            pspair = ps_st if ((cgx - 1) // 2) % 2 == 0 else ps_st2
                            if half == 0 and cgx > 4:
                                t.wait_ge(sq.h, SQ_BY_CG[cgx - 3])
                            mmi = t.matmul(pspair[:, half * 512:half * 512 + 512],
                                           kfm(slot)[64 * half:64 * half + 64, c * 128:(c + 1) * 128],
                                           qfm(slot)[64 * half:64 * half + 64, g_ * 512:(g_ + 1) * 512],
                                           start=True, stop=True, tile_position=(64 * half, 0))
                            if half == 1:
                                mm.inc(mmi)
                                assert mm.n == MMP[(h, g_, c // 2)]
                        for c in (pr0, pr0 + 1):
                            if isdiag(g_, c):
                                t.wait_ge(vq.h, VQP[("msk", h, g_, c)])
                            else:
                                t.wait_ge(sq.h, SQ_EXP[(h, g_, c)])
                            for sub in range(4):
                                ii = t.matmul(ps_av(sub),
                                              ptb(cg_of[(h, g_, c)] % 4)[:, sub * 128:(sub + 1) * 128],
                                              vaug(slot)[:, c * 65:c * 65 + 65],
                                              start=(c == 0), stop=(c == nch_ - 1))
                            avc.inc(ii)
                            assert avc.n == AVCC[(h, g_, c)]

        @block.scalar
        def _(s):
            if stage < 2:
                return
            s.wait_ge(mm.h, MM_EXPB)
            sq.inc(s.activation(expb[:, :], ps_bc[:, :], AF.Copy, scale=-0.125))
            assert sq.n == SQ_EXPB
            if stage < 5:
                return
            for h in range(HPC):
                s.wait_ge(vq.h, VQP[("vhg", h)])
                sq.inc(s.activation(n2col, n2col, AF.Ln))
                s.wait_ge(sq.h, sq.n)
                sq.inc(s.activation(n2col, n2col, AF.Exp, scale=-0.5))
                assert sq.n == SQ_VHEXP[h]
                for g_ in range(NQT):
                    nch_ = chunks(g_)
                    for c in range(nch_):
                        cgx = cg_of[(h, g_, c)]
                        half = (cgx - 1) % 2
                        pspair = ps_st if ((cgx - 1) // 2) % 2 == 0 else ps_st2
                        if half == 0:
                            s.wait_ge(mm.h, MMP[(h, g_, c // 2)])
                        if cgx > 4:
                            s.wait_ge(avc.h, AVC_BY_CG[cgx - 4])
                        sq.inc(s.activation(ptb(cgx % 4), pspair[:, half * 512:half * 512 + 512],
                                            AF.Exp, bias=expb[:, h:h + 1], scale=0.125))
                        assert sq.n == SQ_EXP[(h, g_, c)]
    return nc


ROPE_PERM = np.concatenate([np.concatenate([np.arange(h * 64, h * 64 + 64, 2),
                                            np.arange(h * 64 + 1, h * 64 + 64, 2)])
                            for h in range(H)])


def make_costab(positions, NTI):
    BASE, SCALES = 10000.0, (1.0, 4.0, 16.0)
    ii = np.arange(D // 2, dtype=np.float64)
    coss, sins = [], []
    for s in SCALES:
        freqs = 1.0 / (BASE * s) ** (2.0 * ii / D)
        ang = positions[:, None].astype(np.float64) * freqs[None, :]
        coss.append(np.cos(ang)); sins.append(np.sin(ang))
    cos = np.mean(np.stack(coss), 0).astype(np.float32)
    sin = np.mean(np.stack(sins), 0).astype(np.float32)
    out = np.zeros((128, NTI * 64), np.float32)
    for ti in range(NTI):
        out[:, ti * 64:ti * 64 + 32] = cos[ti * 128:(ti + 1) * 128]
        out[:, ti * 64 + 32:ti * 64 + 64] = sin[ti * 128:(ti + 1) * 128]
    return out


def build_B(TOK=2048, races=True, dbg=False):
    NTI = TOK // 128
    QT = 4
    NQ4 = NTI // QT
    nc = bass.Bass(target_bir_lowering=False, detect_race_conditions=races)
    x_own = nc.dram_tensor("x_own", [TOK, C], F32, kind="ExternalInput").ap()
    y_cat = nc.dram_tensor("y_cat", [TOK, C], BF16, kind="ExternalInput").ap()
    wp = nc.dram_tensor("wp", [C, C], F32, kind="ExternalInput").ap()
    bp = nc.dram_tensor("bp", [1, C], F32, kind="ExternalInput").ap()
    gwT = nc.dram_tensor("gwT", [C, NE], F32, kind="ExternalInput").ap()
    ewT = nc.dram_tensor("ewT", [NE, C, C], F32, kind="ExternalInput").ap()
    wqkv = nc.dram_tensor("wqkv", [C, 3 * C], F32, kind="ExternalInput").ap()
    costab = nc.dram_tensor("costab", [128, NTI * 64], F32, kind="ExternalInput").ap()
    x2_own = nc.dram_tensor("x2_own", [TOK, C], F32, kind="ExternalOutput").ap()
    qrd = nc.dram_tensor("qrd", [128, NTI * H * 128], BF16, kind="ExternalOutput").ap()
    krd = nc.dram_tensor("krd", [128, NTI * H * 128], BF16, kind="ExternalOutput").ap()
    vhpm = nc.dram_tensor("vhpm", [H, 128, NTI * D], BF16, kind="ExternalOutput").ap()
    qn2o = nc.dram_tensor("qn2o", [16, 1], F32, kind="ExternalOutput").ap()
    kn2o = nc.dram_tensor("kn2o", [16, 1], F32, kind="ExternalOutput").ap()
    x1bf = nc.dram_tensor("x1bf", [TOK, C], BF16, kind="ExternalOutput" if dbg else "Internal").ap()
    x1lbf = nc.dram_tensor("x1lbf", [TOK, C], BF16).ap()
    x2bf = nc.dram_tensor("x2bf", [TOK, C], BF16).ap()
    qkvtm = nc.dram_tensor("qkvtm", [TOK, 3 * C], BF16).ap()
    n2d = nc.dram_tensor("n2d", [128, 32], F32).ap()

    VQP = {}
    with (
        nc.sbuf_tensor("arena", [128, 38000], F32) as AR,
        nc.sbuf_tensor("wbuf", [128, 8 * 1024], BF16) as wbuf,
        nc.sbuf_tensor("wqb", [128, 2 * 8 * 512], BF16) as wqb,
        nc.sbuf_tensor("gw_sb", [128, 128], BF16) as gw_sb,
        nc.sbuf_tensor("bias_sb", [1, C], BF16) as bias_sb,
        nc.sbuf_tensor("ones1", [1, 128], BF16) as ones1,
        nc.sbuf_tensor("cs_sb", [128, NTI * 64], F32) as cs_sb,
        nc.sbuf_tensor("n2t", [32, 128], F32) as n2t,
        nc.sbuf_tensor("n2m", [32, 1], F32) as n2m,
        nc.psum_tensor("PS", [128, 4096], F32) as PS,
        ExitStack() as _es,
        nc.Block() as block,
    ):
        _sn = ["ld", "tr", "mm", "sq", "vq", "gd", "st", "wl"]
        ld, tr, mm, sq, vq, gd, st, wl = (S(_es.enter_context(nc.semaphore(n))) for n in _sn)

        o, off = {}, 0

        def alloc(n_, sz):
            nonlocal off
            o[n_] = off
            off += sz
        alloc("xres", NTI * 1024)
        alloc("acc", QT * 1024)
        alloc("fm", NTI * 512)
        alloc("mask", NTI * 8)
        alloc("qk", 3072)
        alloc("rt", 1024)
        alloc("stg", 3072)
        alloc("sc", 512)
        alloc("n2", 32)
        alloc("gwf", 128)
        assert off <= 38000, off
        NQG = NTI // 4
        assert NTI % 4 == 0

        def f32v(name, a, n):
            return AR[:, o[name] + a:o[name] + a + n]

        def bf16v(name, a, n):
            return AR.bitcast(BF16)[:, 2 * o[name] + a:2 * o[name] + a + n]

        def xres(ti):
            return f32v("xres", ti * 1024, 1024)

        def acc(j):
            return f32v("acc", j * 1024, 1024)

        def fmc(d):
            return bf16v("fm", d * TOK, TOK)

        def maskv(ti):
            return f32v("mask", ti * 8, 8)
        qs = f32v("qk", 0, 1024)
        ks = f32v("qk", 1024, 1024)
        vs = f32v("qk", 2048, 1024)
        rt1 = f32v("rt", 0, 512)
        rt2 = f32v("rt", 512, 512)
        PSB = [PS[:, i * 512:(i + 1) * 512] for i in range(8)]
        s1 = f32v("sc", 32, 1)
        s2 = f32v("sc", 33, 1)
        mvar = f32v("sc", 34, 4)
        mx8 = f32v("sc", 40, 1)
        eq8 = f32v("sc", 48, 8)
        cu8 = f32v("sc", 56, 8)
        n2s = f32v("sc", 64, 16)
        sqscr = f32v("qk", 0, 1024)

        def flc(s, dch):
            return bf16v("qk", s * 4096 + dch * 512, 512)

        def lostg(ti):
            return bf16v("stg", 3072 + (ti % 2) * 1024, 1024)

        SQ_SQ, SQ_RSTD = {}, {}
        sq_n = 0
        for ph in range(2):
            for ti in range(NTI):
                SQ_SQ[(ph, ti)] = sq_n + 1
                SQ_RSTD[(ph, ti)] = sq_n + 3
                sq_n += 3
        mmc = [0]

        def mmark():
            mmc[0] += 1
            return mmc[0]
        MM_PJ = {ti: mmark() for ti in range(NTI)}
        MM_G = {ti: mmark() for ti in range(NTI)}
        MM_E = {(q4, e, j): mmark() for q4 in range(NQ4) for e in range(NE) for j in range(QT)}
        MM_QKV = {(b6, ti): mmark() for b6 in range(6) for ti in range(NTI)}
        LD_IN = 16 * 20
        LD_ROPE = {ti: LD_IN + 16 * 3 * (ti + 1) for ti in range(NTI)}
        LD_N2 = LD_ROPE[NTI - 1] + 16 * 2
        WL_E = {}
        wl_n = 0
        for q4 in range(NQ4):
            for e in range(NE):
                wl_n += 16 * 8
                WL_E[(q4, e)] = wl_n
        WL_QKV = {}
        for b6 in range(6):
            wl_n += 16 * 8
            WL_QKV[b6] = wl_n
        TR_Y = 16 * 8
        TR_X1 = TR_Y + 16 * 8
        TR_X1L = {g: TR_X1 + 16 * 8 * (g + 1) for g in range(NQG)}
        TR_X2 = TR_X1L[NQG - 1] + 16 * 8
        ST_X1 = {ti: 32 * ti + 16 for ti in range(NTI)}
        ST_X1L = {ti: 32 * ti + 32 for ti in range(NTI)}
        st_n0 = 32 * NTI
        ST_X2 = {ti: st_n0 + 16 * 2 * (ti + 1) for ti in range(NTI)}
        st_n1 = st_n0 + 16 * 2 * NTI
        ST_QKVMM = {}
        k_ = st_n1
        for b6 in range(6):
            for ti in range(NTI):
                k_ += 16
                ST_QKVMM[(b6, ti)] = k_
        st_n2 = k_
        ST_ROPE = {}
        for ti in range(NTI):
            k_ += 16 * 6
            ST_ROPE[ti] = k_

        @block.vector
        def _(v):
            def F(i):
                return vq.inc(i)

            def W():
                v.wait_ge(vq.h, vq.n)

            def LNtile(ph, ti, dst, src_res):
                W()
                F(v.tensor_reduce(s1, src_res, AX.X, ALU.add))
                VQP[("lnr", ph, ti)] = vq.n
                v.wait_ge(sq.h, SQ_SQ[(ph, ti)])
                W()
                F(v.tensor_scalar(mvar[:, 0:1], s1, 1.0 / 1024, None, ALU.mult))
                W()
                F(v.tensor_scalar(mvar[:, 1:2], s2, 1.0 / 1024, None, ALU.mult))
                W()
                F(v.tensor_tensor(mvar[:, 2:3], mvar[:, 0:1], mvar[:, 0:1], ALU.mult))
                W()
                F(v.tensor_tensor(mvar[:, 2:3], mvar[:, 1:2], mvar[:, 2:3], ALU.subtract))
                W()
                F(v.tensor_scalar(mvar[:, 2:3], mvar[:, 2:3], 1e-5, None, ALU.add))
                VQP[("lnv", ph, ti)] = vq.n
                v.wait_ge(sq.h, SQ_RSTD[(ph, ti)])
                W()
                F(v.tensor_scalar(dst, src_res, mvar[:, 0:1], mvar[:, 3:4], ALU.subtract, ALU.mult))
                VQP[("lnd", ph, ti)] = vq.n

            v.wait_ge(ld.h, LD_IN)
            F(v.tensor_copy(gw_sb[:, 0:64], f32v("gwf", 0, 64)))
            W()
            F(v.tensor_tensor(f32v("gwf", 64, 64), f32v("gwf", 0, 64), gw_sb[:, 0:64], ALU.subtract))
            W()
            F(v.tensor_copy(gw_sb[:, 64:128], f32v("gwf", 64, 64)))
            VQP[("gwp",)] = vq.n
            for ti in range(NTI):
                v.wait_ge(mm.h, MM_PJ[ti])
                pe, po = (PSB[0], PSB[1]) if ti % 2 == 0 else (PSB[2], PSB[3])
                W()
                F(v.tensor_tensor(xres(ti)[:, 0:512], pe, xres(ti)[:, 0:512], ALU.add))
                W()
                F(v.tensor_tensor(xres(ti)[:, 512:1024], po, xres(ti)[:, 512:1024], ALU.add))
                VQP[("pj", ti)] = vq.n
                LNtile(0, ti, xres(ti), xres(ti))
                if ti >= 2:
                    v.wait_ge(st.h, ST_X1[ti - 2])
                W()
                F(v.tensor_copy(bf16v("stg", (ti % 2) * 1024, 1024), xres(ti)))
                VQP[("x1s", ti)] = vq.n
                if ti >= 2:
                    v.wait_ge(st.h, ST_X1L[ti - 2])
                W()
                F(v.tensor_tensor(lostg(ti), xres(ti), bf16v("stg", (ti % 2) * 1024, 1024),
                                  ALU.subtract))
                VQP[("x1l", ti)] = vq.n
            for ti in range(NTI):
                v.wait_ge(mm.h, MM_G[ti])
                W()
                F(v.tensor_reduce(mx8, PSB[4][:, 0:8], AX.X, ALU.max))
                W()
                F(v.tensor_scalar(eq8, PSB[4][:, 0:8], mx8, None, ALU.is_ge))
                W()
                F(v.tensor_copy(cu8, eq8))
                for sh in (1, 2, 4):
                    W()
                    F(v.tensor_tensor(cu8[:, sh:8], cu8[:, sh:8], cu8[:, 0:8 - sh], ALU.add))
                W()
                F(v.tensor_scalar(cu8, cu8, 1.0, None, ALU.is_equal))
                W()
                F(v.tensor_tensor(maskv(ti), eq8, cu8, ALU.mult))
                VQP[("gm", ti)] = vq.n
            for q4 in range(NQ4):
                for j in range(QT):
                    ti = q4 * QT + j
                    if q4 >= 1:
                        v.wait_ge(sq.h, SQ_SQ[(1, (q4 - 1) * QT + j)])
                    W()
                    F(v.tensor_scalar(acc(j), xres(ti), 2.0, None, ALU.mult))
                    VQP[("ac0", q4, j)] = vq.n
                for e in range(NE):
                    for j in range(QT):
                        v.wait_ge(mm.h, MM_E[(q4, e, j)])
                        pe, po = (PSB[0], PSB[1]) if j % 2 == 0 else (PSB[2], PSB[3])
                        W()
                        F(v.scalar_tensor_tensor(acc(j)[:, 0:512], pe, maskv(q4 * QT + j)[:, e:e + 1],
                                                 acc(j)[:, 0:512], ALU.mult, ALU.add))
                        W()
                        F(v.scalar_tensor_tensor(acc(j)[:, 512:1024], po, maskv(q4 * QT + j)[:, e:e + 1],
                                                 acc(j)[:, 512:1024], ALU.mult, ALU.add))
                        VQP[("ex", q4, e, j)] = vq.n
                for j in range(QT):
                    ti = q4 * QT + j
                    LNtile(1, ti, xres(ti), acc(j))
                    if ti >= 2:
                        v.wait_ge(st.h, ST_X2[ti - 2])
                    W()
                    F(v.tensor_copy(bf16v("stg", (ti % 2) * 1024, 1024), xres(ti)))
                    VQP[("x2s", ti)] = vq.n
            for b6 in range(6):
                for ti in range(NTI):
                    v.wait_ge(mm.h, MM_QKV[(b6, ti)])
                    if (b6, ti) > (0, 1):
                        pi = b6 * NTI + ti - 2
                        v.wait_ge(st.h, ST_QKVMM[(pi // NTI, pi % NTI)])
                    pe = PSB[ti % 2]
                    W()
                    F(v.tensor_copy(bf16v("stg", 2048 + (ti % 2) * 512, 512), pe))
                    VQP[("qe", b6, ti)] = vq.n
            for ti in range(NTI):
                v.wait_ge(ld.h, LD_ROPE[ti])
                cs = cs_sb[:, ti * 64:ti * 64 + 32]
                sn = cs_sb[:, ti * 64 + 32:ti * 64 + 64]
                csb = cs.rearrange("p (x w) -> p x w", x=1).broadcast_to((128, 16, 32))
                snb = sn.rearrange("p (x w) -> p x w", x=1).broadcast_to((128, 16, 32))
                for z in (qs, ks):
                    z3 = z.rearrange("p (h d) -> p h d", d=64)
                    ev, ov = z3[:, :, 0:32], z3[:, :, 32:64]
                    t1 = rt1.rearrange("p (h w) -> p h w", w=32)
                    t2 = rt2.rearrange("p (h w) -> p h w", w=32)
                    W()
                    F(v.tensor_tensor(t1, ev, csb, ALU.mult))
                    W()
                    F(v.tensor_tensor(t2, ov, snb, ALU.mult))
                    W()
                    F(v.tensor_tensor(t2, t1, t2, ALU.subtract))
                    W()
                    F(v.tensor_tensor(t1, ev, snb, ALU.mult))
                    W()
                    F(v.tensor_tensor(ov, ov, csb, ALU.mult))
                    W()
                    F(v.tensor_tensor(ov, t1, ov, ALU.add))
                    W()
                    F(v.tensor_copy(ev, t2))
                for z, ncol in ((qs, 0), (ks, 16)):
                    W()
                    F(v.tensor_tensor(f32v("acc", 0, 1024), z, z, ALU.mult))
                    W()
                    F(v.tensor_reduce(n2s, f32v("acc", 0, 1024).rearrange("p (h d) -> p h d", d=64),
                                      AX.X, ALU.add))
                    W()
                    if ti == 0:
                        F(v.tensor_copy(f32v("n2", ncol, 16), n2s))
                    else:
                        F(v.tensor_tensor(f32v("n2", ncol, 16), f32v("n2", ncol, 16), n2s, ALU.max))
                if ti >= 1:
                    v.wait_ge(st.h, ST_ROPE[ti - 1])
                else:
                    v.wait_ge(st.h, ST_X2[NTI - 1])
                st_q = bf16v("stg", 0, 2048).rearrange("p (h x) -> p h x", x=128)
                st_k = bf16v("stg", 3072, 2048).rearrange("p (h x) -> p h x", x=128)
                st_v = bf16v("stg", 5120, 1024)
                q3 = qs.rearrange("p (h d) -> p h d", d=64)
                k3 = ks.rearrange("p (h d) -> p h d", d=64)
                W()
                F(v.tensor_copy(st_q[:, :, 0:64], q3))
                W()
                F(v.tensor_copy(st_q[:, :, 64:128], q3))
                W()
                F(v.tensor_copy(st_k[:, :, 0:64], k3))
                W()
                F(v.tensor_copy(st_k[:, :, 64:128], k3))
                W()
                F(v.tensor_copy(st_v, vs))
                VQP[("rp", ti)] = vq.n
            v.wait_ge(ld.h, LD_N2)
            W()
            F(v.tensor_reduce(n2m[:, :], n2t[:, :], AX.X, ALU.max))
            VQP[("n2",)] = vq.n

        @block.scalar
        def _(s):
            for ph in range(2):
                for tii in range(NTI):
                    ti = tii
                    key = ("lnr", ph, ti)
                    s.wait_ge(vq.h, VQP[key])
                    src = xres(ti) if ph == 0 else acc(ti % QT)
                    sq.inc(s.activation(sqscr, src, AF.Square, accum_out=s2))
                    s.wait_ge(vq.h, VQP[("lnv", ph, ti)])
                    sq.inc(s.activation(mvar[:, 3:4], mvar[:, 2:3], AF.Ln))
                    s.wait_ge(sq.h, sq.n)
                    sq.inc(s.activation(mvar[:, 3:4], mvar[:, 3:4], AF.Exp, scale=-0.5))
                    assert sq.n == SQ_RSTD[(ph, ti)]

        @block.gpsimd
        def _(g):
            ld.inc(g.dma_start(out=AR[:, o["xres"]:o["xres"] + NTI * 1024].rearrange(
                "p (t w) -> p t w", w=1024), in_=x_own.rearrange("(t p) w -> p t w", p=128)), 16)
            for dch in range(8):
                ld.inc(g.dma_start(out=wbuf[:, dch * 1024:(dch + 1) * 1024],
                                   in_=wp[dch * 128:(dch + 1) * 128, :]), 16)
            ld.inc(g.dma_start(out=bias_sb[:, :], in_=bp), 16)
            ld.inc(g.dma_start(out=f32v("gwf", 0, 64).rearrange("p (dch e) -> p dch e", e=NE),
                               in_=gwT.rearrange("(dch p) e -> p dch e", p=128)), 16)
            gd.inc(g.memset(ones1[:, :], 1.0))
            ld.inc(g.dma_start(out=cs_sb[:, :], in_=costab), 16)
            for _pad in range(20 - 12):
                ld.inc(g.dma_start(out=cs_sb[:, 0:1], in_=costab[:, 0:1],
                                   allow_slow_non_contiguous=True), 16)
            assert ld.n == LD_IN
            g.wait_ge(mm.h, MM_G[NTI - 1])
            for q4 in range(NQ4):
                for e in range(NE):
                    if (q4, e) != (0, 0):
                        pe_, pq_ = ((e - 1) % NE, q4 - (1 if e == 0 else 0))
                        g.wait_ge(mm.h, MM_E[(pq_, pe_, QT - 1)])
                    for dch in range(8):
                        wl.inc(g.dma_start(
                            out=wbuf[:, dch * 1024:(dch + 1) * 1024],
                            in_=ewT[e][dch * 128:(dch + 1) * 128, :]), 16)
                    assert wl.n == WL_E[(q4, e)]
                    g.wait_ge(wl.h, wl.n)
            for b6 in range(6):
                if b6 >= 2:
                    g.wait_ge(mm.h, MM_QKV[(b6 - 2, NTI - 1)])
                elif b6 == 0:
                    g.wait_ge(mm.h, MM_E[(NQ4 - 1, NE - 1, QT - 1)])
                for dch in range(8):
                    wl.inc(g.dma_start(
                        out=wqb[:, (b6 % 2) * 4096 + dch * 512:(b6 % 2) * 4096 + (dch + 1) * 512],
                        in_=wqkv[dch * 128:(dch + 1) * 128, b6 * 512:(b6 + 1) * 512]), 16)
                assert wl.n == WL_QKV[b6]
                g.wait_ge(wl.h, wl.n)
            for ti in range(NTI):
                g.wait_ge(st.h, ST_QKVMM[(5, ti)])
                if ti >= 1:
                    g.wait_ge(vq.h, VQP[("rp", ti - 1)])
                ld.inc(g.dma_start(out=qs, in_=qkvtm[ti * 128:(ti + 1) * 128, 0:1024]), 16)
                ld.inc(g.dma_start(out=ks, in_=qkvtm[ti * 128:(ti + 1) * 128, 1024:2048]), 16)
                ld.inc(g.dma_start(out=vs, in_=qkvtm[ti * 128:(ti + 1) * 128, 2048:3072]), 16)
                assert ld.n == LD_ROPE[ti]
                g.wait_ge(ld.h, ld.n)
            g.wait_ge(vq.h, VQP[("rp", NTI - 1)])
            ld.inc(g.dma_start(out=n2d, in_=f32v("n2", 0, 32)), 16)
            g.wait_ge(ld.h, ld.n)
            ld.inc(g.dma_start(out=n2t[:, :], in_=n2d.rearrange("p x -> x p"),
                               allow_slow_non_contiguous=True), 16)
            assert ld.n == LD_N2
            g.wait_ge(ld.h, ld.n)

        @block.sync
        def _(sy):
            for dch in range(8):
                tr.inc(sy.dma_start(out=fmc(dch), in_=y_cat[:, dch * 128:(dch + 1) * 128],
                                    transpose=True), 16)
            sy.wait_ge(tr.h, tr.n)
            assert tr.n == TR_Y
            for ti in range(NTI):
                sy.wait_ge(vq.h, VQP[("x1s", ti)])
                st.inc(sy.dma_start(out=x1bf[ti * 128:(ti + 1) * 128, :],
                                    in_=bf16v("stg", (ti % 2) * 1024, 1024)), 16)
                assert st.n == ST_X1[ti]
                sy.wait_ge(st.h, st.n)
                sy.wait_ge(vq.h, VQP[("x1l", ti)])
                st.inc(sy.dma_start(out=x1lbf[ti * 128:(ti + 1) * 128, :], in_=lostg(ti)), 16)
                assert st.n == ST_X1L[ti]
                sy.wait_ge(st.h, st.n)
            for dch in range(8):
                tr.inc(sy.dma_start(out=fmc(dch), in_=x1bf[:, dch * 128:(dch + 1) * 128],
                                    transpose=True), 16)
            sy.wait_ge(tr.h, tr.n)
            assert tr.n == TR_X1
            for gq in range(NQG):
                if gq >= 2:
                    sy.wait_ge(mm.h, MM_G[4 * (gq - 2) + 3])
                for dch in range(8):
                    tr.inc(sy.dma_start(out=flc(gq % 2, dch),
                                        in_=x1lbf[gq * 512:(gq + 1) * 512, dch * 128:(dch + 1) * 128],
                                        transpose=True), 16)
                assert tr.n == TR_X1L[gq]
                sy.wait_ge(tr.h, tr.n)
            for ti in range(NTI):
                sy.wait_ge(vq.h, VQP[("x2s", ti)])
                st.inc(sy.dma_start(out=x2bf[ti * 128:(ti + 1) * 128, :],
                                    in_=bf16v("stg", (ti % 2) * 1024, 1024)), 16)
                sy.wait_ge(st.h, st.n)
                st.inc(sy.dma_start(out=x2_own[ti * 128:(ti + 1) * 128, :], in_=xres(ti)), 16)
                assert st.n == ST_X2[ti]
                sy.wait_ge(st.h, st.n)
            for dch in range(8):
                tr.inc(sy.dma_start(out=fmc(dch), in_=x2bf[:, dch * 128:(dch + 1) * 128],
                                    transpose=True), 16)
            sy.wait_ge(tr.h, tr.n)
            assert tr.n == TR_X2
            for b6 in range(6):
                for ti in range(NTI):
                    sy.wait_ge(vq.h, VQP[("qe", b6, ti)])
                    st.inc(sy.dma_start(out=qkvtm[ti * 128:(ti + 1) * 128, b6 * 512:(b6 + 1) * 512],
                                        in_=bf16v("stg", 2048 + (ti % 2) * 512, 512)), 16)
                    assert st.n == ST_QKVMM[(b6, ti)]
                    sy.wait_ge(st.h, st.n)
            for ti in range(NTI):
                sy.wait_ge(vq.h, VQP[("rp", ti)])
                st.inc(sy.dma_start(out=qrd[:, ti * 2048:(ti + 1) * 2048],
                                    in_=bf16v("stg", 0, 2048)), 16)
                st.inc(sy.dma_start(out=krd[:, ti * 2048:(ti + 1) * 2048],
                                    in_=bf16v("stg", 3072, 2048)), 16)
                for hg in range(4):
                    st.inc(sy.dma_start(
                        out=vhpm.rearrange("h p (t d) -> p h t d", d=64)[:, hg * 4:(hg + 1) * 4, ti, :],
                        in_=bf16v("stg", 5120, 1024).rearrange("p (h d) -> p h d", d=64)[:, hg * 4:(hg + 1) * 4, :]), 16)
                assert st.n == ST_ROPE[ti]
                sy.wait_ge(st.h, st.n)
            sy.wait_ge(vq.h, VQP[("n2",)])
            st.inc(sy.dma_start(out=qn2o, in_=n2m[0:16, :]), 16)
            st.inc(sy.dma_start(out=kn2o, in_=n2m[16:32, :]), 16)
            sy.wait_ge(st.h, st.n)

        @block.tensor
        def _(t):
            t.wait_ge(ld.h, LD_IN)
            t.wait_ge(gd.h, 1)
            t.wait_ge(tr.h, TR_Y)
            for ti in range(NTI):
                pe, po = (PSB[0], PSB[1]) if ti % 2 == 0 else (PSB[2], PSB[3])
                if ti >= 2:
                    t.wait_ge(vq.h, VQP[("pj", ti - 1)])
                for dch in range(8):
                    t.matmul(pe, fmc(dch)[:, ti * 128:(ti + 1) * 128],
                             wbuf[:, dch * 1024:dch * 1024 + 512], start=(dch == 0), stop=False)
                    t.matmul(po, fmc(dch)[:, ti * 128:(ti + 1) * 128],
                             wbuf[:, dch * 1024 + 512:dch * 1024 + 1024], start=(dch == 0), stop=False)
                t.matmul(pe, ones1[0:1, :], bias_sb[0:1, 0:512], start=False, stop=True)
                mm.inc(t.matmul(po, ones1[0:1, :], bias_sb[0:1, 512:1024], start=False, stop=True))
                assert mm.n == MM_PJ[ti]
            t.wait_ge(tr.h, TR_X1)
            t.wait_ge(vq.h, VQP[("gwp",)])
            for ti in range(NTI):
                if ti >= 1:
                    t.wait_ge(vq.h, VQP[("gm", ti - 1)])
                t.wait_ge(tr.h, TR_X1L[ti // 4])
                for dch in range(8):
                    t.matmul(PSB[4][:, 0:8], fmc(dch)[:, ti * 128:(ti + 1) * 128],
                             gw_sb[:, dch * NE:(dch + 1) * NE],
                             start=(dch == 0), stop=False)
                    t.matmul(PSB[4][:, 0:8], flc((ti // 4) % 2, dch)[:, (ti % 4) * 128:(ti % 4) * 128 + 128],
                             gw_sb[:, dch * NE:(dch + 1) * NE], start=False, stop=False)
                    mmi = t.matmul(PSB[4][:, 0:8], fmc(dch)[:, ti * 128:(ti + 1) * 128],
                                   gw_sb[:, 64 + dch * NE:64 + (dch + 1) * NE],
                                   start=False, stop=(dch == 7))
                mm.inc(mmi)
                assert mm.n == MM_G[ti]
            t.wait_ge(vq.h, VQP[("gm", NTI - 1)])
            for q4 in range(NQ4):
                for e in range(NE):
                    t.wait_ge(wl.h, WL_E[(q4, e)])
                    for j in range(QT):
                        t.wait_ge(vq.h, VQP[("ac0", q4, j)])
                        pe, po = (PSB[0], PSB[1]) if j % 2 == 0 else (PSB[2], PSB[3])
                        gi = (q4 * NE + e) * QT + j
                        if gi >= 2:
                            pgi = gi - 2
                            pq4, pe_, pj = pgi // (NE * QT), (pgi // QT) % NE, pgi % QT
                            t.wait_ge(vq.h, VQP[("ex", pq4, pe_, pj)])
                        for dch in range(8):
                            t.matmul(pe, fmc(dch)[:, (q4 * QT + j) * 128:(q4 * QT + j) * 128 + 128],
                                     wbuf[:, dch * 1024:dch * 1024 + 512],
                                     start=(dch == 0), stop=(dch == 7))
                            mmi = t.matmul(po, fmc(dch)[:, (q4 * QT + j) * 128:(q4 * QT + j) * 128 + 128],
                                           wbuf[:, dch * 1024 + 512:dch * 1024 + 1024],
                                           start=(dch == 0), stop=(dch == 7))
                        mm.inc(mmi)
                        assert mm.n == MM_E[(q4, e, j)]
            t.wait_ge(tr.h, TR_X2)
            for b6 in range(6):
                t.wait_ge(wl.h, WL_QKV[b6])
                for ti in range(NTI):
                    gi = b6 * NTI + ti
                    if gi >= 2:
                        t.wait_ge(vq.h, VQP[("qe", (gi - 2) // NTI, (gi - 2) % NTI)])
                    else:
                        t.wait_ge(vq.h, VQP[("ex", NQ4 - 1, NE - 1, QT - 1)])
                    pe = PSB[ti % 2]
                    for dch in range(8):
                        mmi = t.matmul(pe, fmc(dch)[:, ti * 128:(ti + 1) * 128],
                                       wqb[:, (b6 % 2) * 4096 + dch * 512:(b6 % 2) * 4096 + (dch + 1) * 512],
                                       start=(dch == 0), stop=(dch == 7))
                    mm.inc(mmi)
                    assert mm.n == MM_QKV[(b6, ti)]
    return nc


def build_CD(TOK=2048, TK=8192, races=True, dbg=False, noln=False):
    NTI = TOK // 128
    NQT = TOK // 512
    NKC = TK // 128
    FF = 4096
    NW1 = FF // 128
    NW2 = C // 128
    NU = NW1 + NW2
    nc = bass.Bass(target_bir_lowering=False, detect_race_conditions=races)
    x2_own = nc.dram_tensor("x2_own", [TOK, C], F32, kind="ExternalInput").ap()
    qrd = nc.dram_tensor("qrd", [H, 128, NTI * 128], BF16, kind="ExternalInput").ap()
    kcat = nc.dram_tensor("kcat", [H, 128, NKC * 128], BF16, kind="ExternalInput").ap()
    vcat = nc.dram_tensor("vcat", [H, 128, NKC * D], BF16, kind="ExternalInput").ap()
    qn2 = nc.dram_tensor("qn2", [16, 1], F32, kind="ExternalInput").ap()
    kn2c = nc.dram_tensor("kn2c", [16, 4], F32, kind="ExternalInput").ap()
    wo = nc.dram_tensor("wo", [C, C], F32, kind="ExternalInput").ap()
    w1 = nc.dram_tensor("w1", [FF, C], F32, kind="ExternalInput").ap()
    w2 = nc.dram_tensor("w2", [C, FF], F32, kind="ExternalInput").ap()
    b1 = nc.dram_tensor("b1", [1, FF], F32, kind="ExternalInput").ap()
    b2 = nc.dram_tensor("b2", [1, C], F32, kind="ExternalInput").ap()
    out_own = nc.dram_tensor("out_own", [TOK, C], F32, kind="ExternalOutput").ap()
    _k = "ExternalOutput" if dbg else "Internal"
    y1d = nc.dram_tensor("y1d", [TOK, C], BF16, kind=_k).ap()
    x3bf = nc.dram_tensor("x3bf", [TOK, C], BF16, kind=_k).ap()
    h2d = nc.dram_tensor("h2d", [TOK, FF], BF16, kind=_k).ap()
    bw1d = nc.dram_tensor("bw1d", [FF, C], BF16, kind=_k).ap()
    bw2d = nc.dram_tensor("bw2d", [FF // 1024, C, 1024], BF16, kind=_k).ap()
    m1d = nc.dram_tensor("m1d", [1, 16], F32, kind=_k if dbg else "Internal").ap()
    w2rd = nc.dram_tensor("w2rd", [128, FF * 4], BF16, kind=_k).ap() if dbg else None
    h2fd = nc.dram_tensor("h2fd", [128, 4096], BF16, kind=_k).ap() if dbg else None

    units = [(h, g) for h in range(H) for g in range(NQT)]
    ASSIGN = {h: [] for h in range(H)}
    for u in range(NU):
        ASSIGN[min(u * H // NU, H - 1)].append(u)

    MM_MB = 1
    mm_n = 1
    MMP, SQ_EXP, AVCC = {}, {}, {}
    cg_of = {}
    cg = 0
    sq_n = 3
    avc_n = 0
    for h in range(H):
        for g in range(NQT):
            for c in range(NKC):
                cg += 1
                cg_of[(h, g, c)] = cg
                if c % 2 == 1:
                    mm_n += 1
                    MMP[(h, g, c // 2)] = mm_n
                sq_n += 1
                SQ_EXP[(h, g, c)] = sq_n
                avc_n += 1
                AVCC[(h, g, c)] = avc_n
    SQ_BY_CG = {cg_of[k]: SQ_EXP[k] for k in cg_of}
    AVC_BY_CG = {cg_of[k]: AVCC[k] for k in cg_of}
    MM_PJ = {}
    for ti in range(NTI):
        mm_n += 1
        MM_PJ[ti] = mm_n
    MM_H = {}
    for fg in range(8):
        for ti in range(NTI):
            mm_n += 1
            MM_H[(fg, ti)] = mm_n
    MM_O = {}
    for cgh in range(2):
        for ti in range(NTI):
            mm_n += 1
            MM_O[(cgh, ti)] = mm_n
    SQ_SQ, SQ_RSTD = {}, {}
    for ti in range(NTI):
        SQ_SQ[(0, ti)] = sq_n + 1
        SQ_RSTD[(0, ti)] = sq_n + 3
        sq_n += 3
    SQ_GELU = {}
    for fg in range(8):
        for ti in range(NTI):
            sq_n += 1
            SQ_GELU[(fg, ti)] = sq_n
    for ti in range(NTI):
        SQ_SQ[(1, ti)] = sq_n + 1
        SQ_RSTD[(1, ti)] = sq_n + 3
        sq_n += 3
    LD_IN = 16 * 13
    LD_M = LD_IN + 32
    LD_VAUG = {h: LD_M + 16 * (h + 1) for h in range(H)}
    WL_U = {u: 16 * (u + 1) for u in range(NU)}
    TR_KQ = {h: 32 * (h + 1) for h in range(H)}
    tr_n = TR_KQ[H - 1]
    TR_YFM = tr_n + 16 * 8
    TR_X3 = TR_YFM + 16 * 8
    tr_n = TR_X3
    TR_B1 = {}
    for fg in range(8):
        tr_n += 16 * 8
        TR_B1[fg] = tr_n
    TR_B2, TR_H2 = {}, {}
    for cgh in range(2):
        tr_n += 16 * 32
        TR_B2[cgh] = tr_n
        for ti in range(NTI):
            tr_n += 16
            TR_H2[(cgh, ti)] = tr_n
    ST_Y, ST_BW = {}, {}
    st_n = 0
    for h in range(H):
        for g in range(NQT):
            st_n += 16
            ST_Y[(h, g)] = st_n
        for u in ASSIGN[h]:
            st_n += 16
            ST_BW[u] = st_n
    ST_X3 = {}
    for ti in range(NTI):
        st_n += 16
        ST_X3[ti] = st_n
    ST_H2D = {}
    for fg in range(8):
        for ti in range(NTI):
            st_n += 16
            ST_H2D[(fg, ti)] = st_n
    ST_OUT = {}
    for ti in range(NTI):
        st_n += 16
        ST_OUT[ti] = st_n

    VQP = {}
    with (
        nc.sbuf_tensor("arena", [128, 45500], F32) as AR,
        nc.sbuf_tensor("wbuf", [128, 8 * 1024], BF16) as wbuf,
        nc.sbuf_tensor("bias1", [1, FF], BF16) as bias1,
        nc.sbuf_tensor("bias2", [1, C], BF16) as bias2,
        nc.sbuf_tensor("ones1", [1, 128], BF16) as ones1,
        nc.sbuf_tensor("kn2sb", [16, 4], F32) as kn2sb,
        nc.sbuf_tensor("qn2sb", [16, 1], F32) as qn2sb,
        nc.sbuf_tensor("kn2m", [16, 1], F32) as kn2m,
        nc.sbuf_tensor("m2sb", [16, 1], F32) as m2sb,
        nc.sbuf_tensor("m116", [1, 16], BF16) as m116,
        nc.sbuf_tensor("expb", [128, 16], F32) as expb,
        nc.psum_tensor("PS", [128, 4096], F32) as PS,
        ExitStack() as _es,
        nc.Block() as block,
    ):
        _sn = ["ld", "tr", "mm", "sq", "vq", "gd", "st", "wl", "avc"]
        ld, tr, mm, sq, vq, gd, st, wl, avc = (S(_es.enter_context(nc.semaphore(n))) for n in _sn)
        ps_st, ps_st2 = PS[:, 0:1024], PS[:, 1024:2048]
        ps_bc = PS[:, 2048:2064]
        PP = [PS[:, i * 512:(i + 1) * 512] for i in range(4)]

        def ps_av(sub):
            return PS[:, 2048 + sub * 512:2048 + sub * 512 + 65]

        o, off = {}, 0

        def alloc(n_, sz):
            nonlocal off
            o[n_] = off
            off += (sz + 7) // 8 * 8
        alloc("xres", NTI * 1024)
        alloc("fm", NTI * 512)
        if NTI * 512 >= 5120:
            o["bws"] = o["fm"]
        else:
            alloc("bws", 5120)
        o["wsc"] = o["bws"] + 4096
        H2R = "fm" if NTI * 512 >= 8192 else "bws"
        alloc("kqfm", max(NKC * 64 + TOK // 2, 4096))
        o["rhs1"] = o["kqfm"]
        alloc("vaug", (NKC * 65 + 1) // 2)
        alloc("vstg2", NKC * 32)
        alloc("pt", 1024)
        alloc("yst", 512)
        alloc("wtl", 8192)
        o["w2r"] = o["wtl"]
        alloc("sc", 512)
        assert off <= 45500, off

        def f32v(name, a, n):
            return AR[:, o[name] + a:o[name] + a + n]

        def bf16v(name, a, n):
            return AR.bitcast(BF16)[:, 2 * o[name] + a:2 * o[name] + a + n]

        def xres(ti):
            return f32v("xres", ti * 1024, 1024)

        def fmc(d):
            return bf16v("fm", d * TOK, TOK)
        kfm = bf16v("kqfm", 0, NKC * 128)
        qfm = bf16v("kqfm", NKC * 128, TOK)
        vaug = bf16v("vaug", 0, NKC * 65)

        def ptb(i):
            return bf16v("pt", (i % 4) * 512, 512)

        def ystage(i):
            return bf16v("yst", (i % 4) * 256, 256)

        def x3st(ti):
            return bf16v("pt", (ti % 2) * 1024, 1024)

        def h2ring(i):
            return bf16v("vaug", (i % 4) * 512, 512)

        def rhs1(s, d):
            return bf16v("rhs1", s * 4096 + d * 512, 512)

        def w2r(fc):
            return bf16v("w2r", fc * 512, 512)

        def wtl(s):
            return f32v("wtl", s * 4096, 4096)

        def bws(s, a, n):
            return bf16v("bws", s * 4096 + a, n)
        wscv = f32v("wsc", 0, 1024)
        sqscr = f32v("vstg2", 0, 1024)
        s1 = f32v("sc", 32, 1)
        s2 = f32v("sc", 33, 1)
        mvar = f32v("sc", 34, 4)
        rcp = f32v("sc", 40, 1)
        yn = bf16v("sc", 2 * 48, 64)
        wred = f32v("sc", 80, 1)
        wr4 = f32v("sc", 84, 4)
        ws2x = f32v("sc", 88, 1)
        wns = f32v("sc", 89, 1)

        @block.vector
        def _(v):
            def F(i):
                return vq.inc(i)

            def W():
                v.wait_ge(vq.h, vq.n)

            def LNtile(ph, ti, dst, src_res):
                W()
                F(v.tensor_reduce(s1, src_res, AX.X, ALU.add))
                VQP[("lnr", ph, ti)] = vq.n
                v.wait_ge(sq.h, SQ_SQ[(ph, ti)])
                W()
                F(v.tensor_scalar(mvar[:, 0:1], s1, 1.0 / 1024, None, ALU.mult))
                W()
                F(v.tensor_scalar(mvar[:, 1:2], s2, 1.0 / 1024, None, ALU.mult))
                W()
                F(v.tensor_tensor(mvar[:, 2:3], mvar[:, 0:1], mvar[:, 0:1], ALU.mult))
                W()
                F(v.tensor_tensor(mvar[:, 2:3], mvar[:, 1:2], mvar[:, 2:3], ALU.subtract))
                W()
                F(v.tensor_scalar(mvar[:, 2:3], mvar[:, 2:3], 1e-5, None, ALU.add))
                VQP[("lnv", ph, ti)] = vq.n
                v.wait_ge(sq.h, SQ_RSTD[(ph, ti)])
                W()
                F(v.tensor_scalar(dst, src_res, mvar[:, 0:1], mvar[:, 3:4], ALU.subtract, ALU.mult))
                VQP[("lnd", ph, ti)] = vq.n

            def binar(u):
                v.wait_ge(wl.h, WL_U[u])
                if u >= 2:
                    v.wait_ge(st.h, ST_BW[u - 2])
                s_ = u % 2
                if u < NW1:
                    wt = wtl(s_)[:, 0:1024]
                    W()
                    F(v.tensor_reduce(wred, wt, AX.X, ALU.add, apply_absolute_value=True))
                    W()
                    F(v.tensor_scalar(ws2x, wred, 2.0 / 1024, None, ALU.mult))
                    W()
                    F(v.tensor_scalar(wns, wred, -1.0 / 1024, None, ALU.mult))
                    W()
                    F(v.tensor_scalar(wscv, wt, 0.0, None, ALU.is_ge))
                    W()
                    F(v.tensor_scalar(bws(s_, 0, 1024), wscv, ws2x, wns, ALU.mult, ALU.add))
                else:
                    wt = wtl(s_)
                    for q in range(4):
                        W()
                        F(v.tensor_reduce(wr4[:, q:q + 1], wt[:, q * 1024:(q + 1) * 1024],
                                          AX.X, ALU.add, apply_absolute_value=True))
                    W()
                    F(v.tensor_reduce(wred, wr4, AX.X, ALU.add))
                    W()
                    F(v.tensor_scalar(ws2x, wred, 2.0 / 4096, None, ALU.mult))
                    W()
                    F(v.tensor_scalar(wns, wred, -1.0 / 4096, None, ALU.mult))
                    for q in range(4):
                        W()
                        F(v.tensor_scalar(wscv, wt[:, q * 1024:(q + 1) * 1024], 0.0, None, ALU.is_ge))
                        W()
                        F(v.tensor_scalar(bws(s_, q * 1024, 1024), wscv, ws2x, wns,
                                          ALU.mult, ALU.add))
                VQP[("bw", u)] = vq.n

            v.wait_ge(ld.h, LD_IN)
            F(v.tensor_reduce(kn2m[:, :], kn2sb[:, :], AX.X, ALU.max))
            W()
            F(v.tensor_tensor(m2sb[:, :], kn2m[:, :], qn2sb[:, :], ALU.mult))
            VQP[("m2",)] = vq.n
            v.wait_ge(gd.h, 1)
            for h in range(H):
                v.wait_ge(ld.h, LD_VAUG[h])
                vdat = vaug.rearrange("p (c w) -> p c w", w=65)[:, :, 0:64]
                W()
                F(v.tensor_copy(vdat, bf16v("vstg2", 0, NKC * 64).rearrange(
                    "p (c w) -> p c w", w=64)))
                VQP[("vld", h)] = vq.n
                for g_ in range(NQT):
                    v.wait_ge(avc.h, AVCC[(h, g_, NKC - 1)])
                    k_ = h * NQT + g_
                    if k_ >= 4:
                        v.wait_ge(st.h, ST_Y[units[(k_ - 4) | 1]])
                    for sub in range(4):
                        W()
                        F(v.reciprocal(rcp, ps_av(sub)[:, 64:65]))
                        W()
                        F(v.tensor_scalar(ystage(k_)[:, sub * 64:(sub + 1) * 64],
                                          ps_av(sub)[:, 0:64], rcp, None, ALU.mult))
                    VQP[("yd", h, g_)] = vq.n
                for u in ASSIGN[h]:
                    binar(u)
            for ti in range(NTI):
                v.wait_ge(mm.h, MM_PJ[ti])
                pe, po = (PP[0], PP[1]) if ti % 2 == 0 else (PP[2], PP[3])
                W()
                F(v.tensor_tensor(xres(ti)[:, 0:512], pe, xres(ti)[:, 0:512], ALU.add))
                W()
                F(v.tensor_tensor(xres(ti)[:, 512:1024], po, xres(ti)[:, 512:1024], ALU.add))
                VQP[("pj", ti)] = vq.n
                LNtile(0, ti, xres(ti), xres(ti))
                if ti >= 2:
                    v.wait_ge(st.h, ST_X3[ti - 2])
                W()
                F(v.tensor_copy(x3st(ti), xres(ti)))
                VQP[("x3s", ti)] = vq.n
            for cgh in range(2):
                for ti in range(NTI):
                    v.wait_ge(mm.h, MM_O[(cgh, ti)])
                    gi = cgh * NTI + ti
                    pe = PP[gi % 2]
                    W()
                    F(v.tensor_tensor(xres(ti)[:, cgh * 512:cgh * 512 + 512], pe,
                                      xres(ti)[:, cgh * 512:cgh * 512 + 512], ALU.add))
                    VQP[("oe", gi)] = vq.n
                    if cgh == 1:
                        if noln:
                            VQP[("lnd", 1, ti)] = vq.n
                        else:
                            LNtile(1, ti, xres(ti), xres(ti))

        @block.scalar
        def _(s):
            s.wait_ge(vq.h, VQP[("m2",)])
            sq.inc(s.activation(m2sb[:, :], m2sb[:, :], AF.Ln))
            s.wait_ge(sq.h, sq.n)
            sq.inc(s.activation(m2sb[:, :], m2sb[:, :], AF.Exp, scale=0.5))
            s.wait_ge(mm.h, MM_MB)
            sq.inc(s.activation(expb[:, :], ps_bc[:, :], AF.Copy, scale=-0.125))
            s.wait_ge(sq.h, sq.n)
            assert sq.n == 3
            for h in range(H):
                for g_ in range(NQT):
                    for c in range(NKC):
                        cgx = cg_of[(h, g_, c)]
                        half = (cgx - 1) % 2
                        pspair = ps_st if ((cgx - 1) // 2) % 2 == 0 else ps_st2
                        if half == 0:
                            s.wait_ge(mm.h, MMP[(h, g_, c // 2)])
                        if cgx > 4:
                            s.wait_ge(avc.h, AVC_BY_CG[cgx - 4])
                        sq.inc(s.activation(ptb(cgx), pspair[:, half * 512:half * 512 + 512],
                                            AF.Exp, bias=expb[:, h:h + 1], scale=0.125))
                        assert sq.n == SQ_EXP[(h, g_, c)]
            for ti in range(NTI):
                s.wait_ge(vq.h, VQP[("lnr", 0, ti)])
                sq.inc(s.activation(sqscr, xres(ti), AF.Square, accum_out=s2))
                s.wait_ge(vq.h, VQP[("lnv", 0, ti)])
                sq.inc(s.activation(mvar[:, 3:4], mvar[:, 2:3], AF.Ln))
                s.wait_ge(sq.h, sq.n)
                sq.inc(s.activation(mvar[:, 3:4], mvar[:, 3:4], AF.Exp, scale=-0.5))
                assert sq.n == SQ_RSTD[(0, ti)]
            for fg in range(8):
                for ti in range(NTI):
                    k_ = fg * NTI + ti
                    s.wait_ge(mm.h, MM_H[(fg, ti)])
                    if k_ >= 4:
                        pk = k_ - 4
                        s.wait_ge(st.h, ST_H2D[(pk // NTI, pk % NTI)])
                    sq.inc(s.activation(h2ring(k_), PP[k_ % 2], AF.Gelu))
                    assert sq.n == SQ_GELU[(fg, ti)]
            if not noln:
                for ti in range(NTI):
                    s.wait_ge(vq.h, VQP[("lnr", 1, ti)])
                    sq.inc(s.activation(sqscr, xres(ti), AF.Square, accum_out=s2))
                    s.wait_ge(vq.h, VQP[("lnv", 1, ti)])
                    sq.inc(s.activation(mvar[:, 3:4], mvar[:, 2:3], AF.Ln))
                    s.wait_ge(sq.h, sq.n)
                    sq.inc(s.activation(mvar[:, 3:4], mvar[:, 3:4], AF.Exp, scale=-0.5))
                    assert sq.n == SQ_RSTD[(1, ti)]

        @block.gpsimd
        def _(g):
            ld.inc(g.dma_start(out=AR[:, o["xres"]:o["xres"] + NTI * 1024].rearrange(
                "p (t w) -> p t w", w=1024), in_=x2_own.rearrange("(t p) w -> p t w", p=128)), 16)
            for dch in range(8):
                ld.inc(g.dma_start(out=wbuf[:, dch * 1024:(dch + 1) * 1024],
                                   in_=wo[dch * 128:(dch + 1) * 128, :]), 16)
            ld.inc(g.dma_start(out=bias1[:, :], in_=b1), 16)
            ld.inc(g.dma_start(out=bias2[:, :], in_=b2), 16)
            ld.inc(g.dma_start(out=kn2sb[:, :], in_=kn2c), 16)
            ld.inc(g.dma_start(out=qn2sb[:, :], in_=qn2), 16)
            assert ld.n == LD_IN
            gd.inc(g.memset(ones1[:, :], 1.0))
            g.wait_ge(gd.h, gd.n)
            g.wait_ge(sq.h, 2)
            ld.inc(g.dma_start(out=m1d.rearrange("x h -> h x"), in_=m2sb[:, :],
                               allow_slow_non_contiguous=True), 16)
            g.wait_ge(ld.h, ld.n)
            ld.inc(g.dma_start(out=m116[:, :], in_=m1d), 16)
            assert ld.n == LD_M
            g.wait_ge(ld.h, ld.n)
            for h in range(H):
                if h >= 1:
                    g.wait_ge(avc.h, AVCC[(h - 1, NQT - 1, NKC - 1)])
                gd.inc(g.memset(vaug, 1.0))
                g.wait_ge(gd.h, gd.n)
                ld.inc(g.dma_start(out=bf16v("vstg2", 0, NKC * 64), in_=vcat[h]), 16)
                assert ld.n == LD_VAUG[h]
                g.wait_ge(ld.h, ld.n)
                for u in ASSIGN[h]:
                    if u >= 2:
                        g.wait_ge(vq.h, VQP[("bw", u - 2)])
                    if u < NW1:
                        wl.inc(g.dma_start(out=wtl(u % 2)[:, 0:1024],
                                           in_=w1[u * 128:(u + 1) * 128, :]), 16)
                    else:
                        ci = u - NW1
                        wl.inc(g.dma_start(out=wtl(u % 2),
                                           in_=w2[ci * 128:(ci + 1) * 128, :]), 16)
                    assert wl.n == WL_U[u]
                    g.wait_ge(wl.h, wl.n)

        @block.sync
        def _(sy):
            def emit_y(h):
                for g_ in range(NQT):
                    k_ = h * NQT + g_
                    sy.wait_ge(vq.h, VQP[("yd", h, g_)])
                    st.inc(sy.dma_start(
                        out=y1d[g_ * 512:(g_ + 1) * 512, h * 64:(h + 1) * 64].rearrange(
                            "(s p) d -> p s d", p=128),
                        in_=ystage(k_).rearrange("p (s d) -> p s d", d=64)), 16)
                    assert st.n == ST_Y[(h, g_)]
                    if k_ % 2 == 1:
                        sy.wait_ge(st.h, st.n)

            def emit_bw(h):
                for u in ASSIGN[h]:
                    sy.wait_ge(vq.h, VQP[("bw", u)])
                    if u < NW1:
                        st.inc(sy.dma_start(out=bw1d[u * 128:(u + 1) * 128, :],
                                            in_=bws(u % 2, 0, 1024)), 16)
                    else:
                        ci = u - NW1
                        st.inc(sy.dma_start(
                            out=bw2d[:, ci * 128:(ci + 1) * 128, :].rearrange("q p f -> p q f"),
                            in_=bws(u % 2, 0, 4096).rearrange("p (q f) -> p q f", q=4)), 16)
                    assert st.n == ST_BW[u]
                    sy.wait_ge(st.h, st.n)

            for h in range(H):
                if h >= 1:
                    emit_y(h - 1)
                    emit_bw(h - 1)
                    sy.wait_ge(vq.h, VQP.get(("yd", h - 1, NQT - 1), 0))
                tr.inc(sy.dma_start(out=kfm.rearrange("x (c p) -> x c p", p=128),
                                    in_=kcat[h].rearrange("p (c x) -> p c x", x=128),
                                    transpose=True), 16)
                tr.inc(sy.dma_start(out=qfm.rearrange("x (c p) -> x c p", p=128),
                                    in_=qrd[h].rearrange("p (c x) -> p c x", x=128),
                                    transpose=True), 16)
                assert tr.n == TR_KQ[h]
                sy.wait_ge(tr.h, tr.n)
            emit_y(H - 1)
            emit_bw(H - 1)
            for dch in range(8):
                tr.inc(sy.dma_start(out=fmc(dch), in_=y1d[:, dch * 128:(dch + 1) * 128],
                                    transpose=True), 16)
            assert tr.n == TR_YFM
            sy.wait_ge(tr.h, tr.n)
            for ti in range(NTI):
                sy.wait_ge(vq.h, VQP[("x3s", ti)])
                st.inc(sy.dma_start(out=x3bf[ti * 128:(ti + 1) * 128, :], in_=x3st(ti)), 16)
                assert st.n == ST_X3[ti]
                sy.wait_ge(st.h, st.n)
            sy.wait_ge(mm.h, MM_PJ[NTI - 1])
            for dch in range(8):
                tr.inc(sy.dma_start(out=fmc(dch), in_=x3bf[:, dch * 128:(dch + 1) * 128],
                                    transpose=True), 16)
            assert tr.n == TR_X3
            sy.wait_ge(tr.h, tr.n)
            for fg in range(8):
                if fg >= 2:
                    sy.wait_ge(mm.h, MM_H[(fg - 2, NTI - 1)])
                for dch in range(8):
                    tr.inc(sy.dma_start(out=rhs1(fg % 2, dch),
                                        in_=bw1d[fg * 512:(fg + 1) * 512, dch * 128:(dch + 1) * 128],
                                        transpose=True), 16)
                assert tr.n == TR_B1[fg]
                sy.wait_ge(tr.h, tr.n)
                for ti in range(NTI):
                    k_ = fg * NTI + ti
                    sy.wait_ge(sq.h, SQ_GELU[(fg, ti)])
                    st.inc(sy.dma_start(out=h2d[ti * 128:(ti + 1) * 128, fg * 512:(fg + 1) * 512],
                                        in_=h2ring(k_)), 16)
                    assert st.n == ST_H2D[(fg, ti)]
                    sy.wait_ge(st.h, st.n)
            for cgh in range(2):
                if cgh == 1:
                    sy.wait_ge(mm.h, MM_O[(0, NTI - 1)])
                for fq in range(4):
                    for fcl in range(8):
                        tr.inc(sy.dma_start(
                            out=w2r(fq * 8 + fcl),
                            in_=bw2d[fq][cgh * 512:(cgh + 1) * 512, fcl * 128:(fcl + 1) * 128],
                            transpose=True), 16)
                    sy.wait_ge(tr.h, tr.n)
                assert tr.n == TR_B2[cgh]
                sy.wait_ge(tr.h, tr.n)
                for ti in range(NTI):
                    gi = cgh * NTI + ti
                    if gi < 2:
                        sy.wait_ge(mm.h, MM_H[(7, NTI - 1)])
                    else:
                        pgi = gi - 2
                        sy.wait_ge(mm.h, MM_O[(pgi // NTI, pgi % NTI)])
                    tr.inc(sy.dma_start(
                        out=bf16v(H2R, (gi % 2) * 4096, 4096).rearrange("x (c p) -> x c p", p=128),
                        in_=h2d[ti * 128:(ti + 1) * 128, :].rearrange("p (c x) -> p c x", x=128),
                        transpose=True), 16)
                    assert tr.n == TR_H2[(cgh, ti)]
                    sy.wait_ge(tr.h, tr.n)
                    if cgh == 1:
                        sy.wait_ge(vq.h, VQP[("lnd", 1, ti)])
                        st.inc(sy.dma_start(out=out_own[ti * 128:(ti + 1) * 128, :],
                                            in_=xres(ti)), 16)
                        assert st.n == ST_OUT[ti]
                        sy.wait_ge(st.h, st.n)
            if dbg:
                st.inc(sy.dma_start(out=w2rd, in_=bf16v("w2r", 0, FF * 4)), 16)
                sy.wait_ge(st.h, st.n)
                st.inc(sy.dma_start(out=h2fd, in_=bf16v(H2R, (((2 * NTI - 1) % 2)) * 4096, 4096)), 16)
                sy.wait_ge(st.h, st.n)

        @block.tensor
        def _(t):
            t.wait_ge(ld.h, LD_M)
            t.wait_ge(gd.h, 1)
            mm.inc(t.matmul(ps_bc, ones1[0:1, :], m116[0:1, :], start=True, stop=True))
            assert mm.n == MM_MB
            for h in range(H):
                t.wait_ge(tr.h, TR_KQ[h])
                t.wait_ge(vq.h, VQP[("vld", h)])
                for g_ in range(NQT):
                    if (h, g_) != (0, 0):
                        prev = units[units.index((h, g_)) - 1]
                        t.wait_ge(vq.h, VQP[("yd", prev[0], prev[1])])
                    for pr0 in range(0, NKC, 2):
                        for c in (pr0, pr0 + 1):
                            cgx = cg_of[(h, g_, c)]
                            half = (cgx - 1) % 2
                            pspair = ps_st if ((cgx - 1) // 2) % 2 == 0 else ps_st2
                            if half == 0 and cgx > 4:
                                t.wait_ge(sq.h, SQ_BY_CG[cgx - 3])
                            mmi = t.matmul(pspair[:, half * 512:half * 512 + 512],
                                           kfm[64 * half:64 * half + 64, c * 128:(c + 1) * 128],
                                           qfm[64 * half:64 * half + 64, g_ * 512:(g_ + 1) * 512],
                                           start=True, stop=True, tile_position=(64 * half, 0))
                            if half == 1:
                                mm.inc(mmi)
                                assert mm.n == MMP[(h, g_, c // 2)]
                        for c in (pr0, pr0 + 1):
                            t.wait_ge(sq.h, SQ_EXP[(h, g_, c)])
                            for sub in range(4):
                                ii = t.matmul(ps_av(sub),
                                              ptb(cg_of[(h, g_, c)])[:, sub * 128:(sub + 1) * 128],
                                              vaug[:, c * 65:c * 65 + 65],
                                              start=(c == 0), stop=(c == NKC - 1))
                            avc.inc(ii)
                            assert avc.n == AVCC[(h, g_, c)]
            t.wait_ge(tr.h, TR_YFM)
            t.wait_ge(sq.h, SQ_EXP[(H - 1, NQT - 1, NKC - 1)])
            for ti in range(NTI):
                pe, po = (PP[0], PP[1]) if ti % 2 == 0 else (PP[2], PP[3])
                if ti >= 2:
                    t.wait_ge(vq.h, VQP[("pj", ti - 1)])
                for dch in range(8):
                    t.matmul(pe, fmc(dch)[:, ti * 128:(ti + 1) * 128],
                             wbuf[:, dch * 1024:dch * 1024 + 512], start=(dch == 0), stop=(dch == 7))
                    mmi = t.matmul(po, fmc(dch)[:, ti * 128:(ti + 1) * 128],
                                   wbuf[:, dch * 1024 + 512:dch * 1024 + 1024],
                                   start=(dch == 0), stop=(dch == 7))
                mm.inc(mmi)
                assert mm.n == MM_PJ[ti]
            t.wait_ge(tr.h, TR_X3)
            for fg in range(8):
                t.wait_ge(tr.h, TR_B1[fg])
                for ti in range(NTI):
                    k_ = fg * NTI + ti
                    pe = PP[k_ % 2]
                    if k_ >= 2:
                        pk = k_ - 2
                        t.wait_ge(sq.h, SQ_GELU[(pk // NTI, pk % NTI)])
                    else:
                        t.wait_ge(vq.h, VQP[("pj", NTI - 1)])
                    for dch in range(8):
                        t.matmul(pe, fmc(dch)[:, ti * 128:(ti + 1) * 128],
                                 rhs1(fg % 2, dch), start=(dch == 0), stop=False)
                    mm.inc(t.matmul(pe, ones1[0:1, :], bias1[0:1, fg * 512:(fg + 1) * 512],
                                    start=False, stop=True))
                    assert mm.n == MM_H[(fg, ti)]
            for cgh in range(2):
                t.wait_ge(tr.h, TR_B2[cgh])
                for ti in range(NTI):
                    gi = cgh * NTI + ti
                    pe = PP[gi % 2]
                    t.wait_ge(tr.h, TR_H2[(cgh, ti)])
                    if gi >= 2:
                        t.wait_ge(vq.h, VQP[("oe", gi - 2)])
                    else:
                        t.wait_ge(sq.h, SQ_GELU[(7, NTI - 1)])
                    for fc in range(FF // 128):
                        t.matmul(pe, bf16v(H2R, (gi % 2) * 4096, 4096)[:, fc * 128:(fc + 1) * 128],
                                 w2r(fc), start=(fc == 0), stop=False)
                    mm.inc(t.matmul(pe, ones1[0:1, :], bias2[0:1, cgh * 512:(cgh + 1) * 512],
                                    start=False, stop=True))
                    assert mm.n == MM_O[(cgh, ti)]
    return nc



B_, T_ = 2, 8192
TOK = 2048
NTI = TOK // 128


def _corelists():
    ov = os.environ.get("KMAP", "")
    if ov:
        g = ov.split()
        return [[int(x) for x in grp.split(",")] for grp in g]
    return [[0, 1, 2, 3], [4, 5, 6, 7]]


def _launch(nc, ims):
    res = [None] * len(ims)
    for grp_i, cores in enumerate(_corelists()):
        sel = list(range(grp_i * 4, grp_i * 4 + 4))
        r = run_bass_kernel_spmd(nc, [ims[s] for s in sel], core_ids=cores)
        for j, s in enumerate(sel):
            res[s] = r.results[j]
    return res


def kernel(L0_attn_x, L0_attn_W_attn, L0_attn_b_attn, L0_attn_W_proj, L0_attn_b_proj,
           L0_ffn_gate_w, L0_ffn_expert_w, L1_attn_Wq, L1_attn_Wk, L1_attn_Wv,
           L1_attn_Wo, L1_ffn_w1, L1_ffn_b1, L1_ffn_w2, L1_ffn_b2):
    x = np.ascontiguousarray(np.asarray(L0_attn_x, dtype=np.float32))
    Wa = np.asarray(L0_attn_W_attn, dtype=np.float32)
    ba = np.asarray(L0_attn_b_attn, dtype=np.float32)
    Wp = np.ascontiguousarray(np.asarray(L0_attn_W_proj, dtype=np.float32))
    bp_ = np.asarray(L0_attn_b_proj, dtype=np.float32)
    gw = np.asarray(L0_ffn_gate_w, dtype=np.float32)
    ew = np.asarray(L0_ffn_expert_w, dtype=np.float32)
    Wq = np.asarray(L1_attn_Wq, dtype=np.float32)
    Wk = np.asarray(L1_attn_Wk, dtype=np.float32)
    Wv = np.asarray(L1_attn_Wv, dtype=np.float32)
    Wo = np.ascontiguousarray(np.asarray(L1_attn_Wo, dtype=np.float32))
    w1 = np.ascontiguousarray(np.asarray(L1_ffn_w1, dtype=np.float32))
    b1 = np.asarray(L1_ffn_b1, dtype=np.float32)
    w2 = np.ascontiguousarray(np.asarray(L1_ffn_w2, dtype=np.float32))
    b2 = np.asarray(L1_ffn_b2, dtype=np.float32)

    ncA = build_A(T_, races=False)
    imsA = []
    for c in range(8):
        b = c // 4
        heads = list(range(4 * (c % 4), 4 * (c % 4) + 4))
        cols = []
        for blk in range(3):
            for h in heads:
                cols.extend(range(blk * C + h * D, blk * C + (h + 1) * D))
        cols = np.array(cols)
        imsA.append({"xfull": x[b],
                     "wqkv": np.ascontiguousarray(Wa[:, cols]),
                     "bqkv": np.ascontiguousarray(ba[cols][None, :])})
    resA = _launch(ncA, imsA)

    ncB = build_B(TOK, races=False)
    wqkv_b = np.ascontiguousarray(np.concatenate(
        [Wq[:, ROPE_PERM], Wk[:, ROPE_PERM], Wv], axis=1))
    gwT = np.ascontiguousarray(gw.T)
    ewT = np.ascontiguousarray(np.swapaxes(ew, 1, 2))
    bp2 = np.ascontiguousarray(bp_[None, :])
    imsB = []
    for c in range(8):
        b, part = c // 4, c % 4
        r0 = part * TOK
        y_cat = np.empty((TOK, C), dtype=ml_dtypes.bfloat16)
        for hg in range(4):
            yo = np.asarray(resA[4 * b + hg]["y_out"])
            for hh in range(4):
                h = hg * 4 + hh
                y_cat[:, h * 64:(h + 1) * 64] = yo[hh, r0:r0 + TOK]
        imsB.append({"x_own": np.ascontiguousarray(x[b, r0:r0 + TOK]),
                     "y_cat": y_cat, "wp": Wp, "bp": bp2, "gwT": gwT, "ewT": ewT,
                     "wqkv": wqkv_b,
                     "costab": make_costab(np.arange(r0, r0 + TOK), NTI)})
    resB = _launch(ncB, imsB)

    ncCD = build_CD(TOK, T_, races=False)

    def _hm(a):
        return np.ascontiguousarray(
            np.asarray(a).reshape(128, NTI, H, 128).transpose(2, 0, 1, 3).reshape(H, 128, NTI * 128))
    kcats, vcats, kn2s = {}, {}, {}
    for b in range(2):
        kcats[b] = np.concatenate([_hm(resB[4 * b + j]["krd"]) for j in range(4)], axis=2)
        vcats[b] = np.ascontiguousarray(np.concatenate(
            [np.asarray(resB[4 * b + j]["vhpm"]) for j in range(4)], axis=2))
        kn2s[b] = np.ascontiguousarray(np.concatenate(
            [np.asarray(resB[4 * b + j]["kn2o"]) for j in range(4)], axis=1))
    imsCD = []
    for c in range(8):
        b = c // 4
        imsCD.append({"x2_own": np.ascontiguousarray(np.asarray(resB[c]["x2_own"])),
                      "qrd": _hm(resB[c]["qrd"]),
                      "kcat": kcats[b], "vcat": vcats[b],
                      "qn2": np.ascontiguousarray(np.asarray(resB[c]["qn2o"])),
                      "kn2c": kn2s[b],
                      "wo": Wo, "w1": w1, "w2": w2,
                      "b1": np.ascontiguousarray(b1[None, :]),
                      "b2": np.ascontiguousarray(b2[None, :])})
    resCD = _launch(ncCD, imsCD)

    out = np.empty((B_, T_, C), np.float32)
    for c in range(8):
        b, part = c // 4, c % 4
        out[b, part * TOK:(part + 1) * TOK] = np.asarray(resCD[c]["out_own"])
    return out
